# version 08
# baseline (speedup 1.0000x reference)
import numpy as np
import concourse.bass as bass
import concourse.mybir as mybir
from concourse.bass_utils import run_bass_kernel_spmd
from contextlib import ExitStack

F32 = mybir.dt.float32
BF16 = mybir.dt.bfloat16
AF = mybir.ActivationFunctionType
ALU = mybir.AluOpType
AX = mybir.AxisListType

NCORES = 8
TPC = 2048
NT = TPC // 128
D = 4096
DC = D // 128
DFF = 11008
FC = DFF // 128
H = 8
SLAB = 1024
NSLAB = D // SLAB
FBS = [2] + [7] * 12
NGRP = len(FBS)
NSLOT = 14
OPT_P4B = True
OPT_P4E = True
EPS_LN = 1e-5
EPS_DEN = 1e-6


def build_nc(dbg=False):
    nc = bass.Bass("TRN2", target_bir_lowering=False)
    x = nc.dram_tensor("x", [TPC, D], F32, kind="ExternalInput").ap()
    wq = nc.dram_tensor("wq", [H, 128, DC, 128], F32, kind="ExternalInput").ap()
    bq = nc.dram_tensor("bq", [128, H], F32, kind="ExternalInput").ap()
    zt = nc.dram_tensor("zt", [128, H], F32, kind="ExternalInput").ap()
    mm = nc.dram_tensor("mm", [H, 128, 128], F32, kind="ExternalInput").ap()
    wo = nc.dram_tensor("wo", [H, 4, 128, D], F32, kind="ExternalInput").ap()
    wg = nc.dram_tensor("wg", [FC, 128, DC, 128], F32, kind="ExternalInput").ap()
    wu = nc.dram_tensor("wu", [FC, 128, DC, 128], F32, kind="ExternalInput").ap()
    wd = nc.dram_tensor("wd", [FC, 128, D], F32, kind="ExternalInput").ap()
    out = nc.dram_tensor("out", [TPC, D], F32, kind="ExternalOutput").ap()

    scratch_kind = "ExternalOutput" if dbg else "Internal"
    hT = nc.dram_tensor("hT_s", [FC, 128, TPC], BF16, kind=scratch_kind).ap()
    woe = nc.dram_tensor("woe_s", [H, 128, D], BF16, kind=scratch_kind).ap()
    retr = nc.dram_tensor("retr_s", [128, H * TPC], BF16, kind=scratch_kind).ap()

    with ExitStack() as top:
        E = top.enter_context
        ident = E(nc.sbuf_tensor("ident", [128, 128], BF16))
        ones1 = E(nc.sbuf_tensor("ones1", [1, 128], BF16))
        bqT = E(nc.sbuf_tensor("bqT", [128, H], F32))
        zT = E(nc.sbuf_tensor("zT", [128, H], BF16))
        M_s = E(nc.sbuf_tensor("M_s", [128, H, 128], BF16))
        epsb = E(nc.sbuf_tensor("epsb", [1, 1], F32))
        s0 = E(nc.semaphore("s0"))

        with nc.Block() as blk:
            @blk.gpsimd
            def _(g):
                g.dma_start(zT[:, :], zt).then_inc(s0, 16)
                g.dma_start(M_s[:, :, :], mm.rearrange("h d e -> d h e")).then_inc(s0, 16)
                g.memset(ident[:, :], 0.0)
                g.affine_select(out=ident[:, :], in_=ident[:, :],
                                compare_op=ALU.not_equal, fill=1.0, base=0,
                                pattern=[[-1, 128]], channel_multiplier=1)
                g.memset(ones1[:, :], 1.0)
                g.memset(epsb[:, :], EPS_DEN)
                g.wait_ge(s0, 48)

            @blk.sync
            def _(sy):
                sy.dma_start(bqT[:, :], bq).then_inc(s0, 16)

        with ExitStack() as mid:
            xnT = mid.enter_context(nc.sbuf_tensor("xnT", [128, DC, TPC], BF16))

            with ExitStack() as st1:
                E1 = st1.enter_context
                xbuf = E1(nc.sbuf_tensor("xbuf", [128, 3, D], F32))
                sqs = E1(nc.sbuf_tensor("sqs", [128, D // 2], F32))
                xnb = E1(nc.sbuf_tensor("xnb", [128, 2, D], BF16))
                ssum = E1(nc.sbuf_tensor("ssum", [128, NT], F32))
                ssqA = E1(nc.sbuf_tensor("ssqA", [128, NT], F32))
                ssqB = E1(nc.sbuf_tensor("ssqB", [128, NT], F32))
                sqv = E1(nc.sbuf_tensor("sqv", [128, NT], F32))
                stt = E1(nc.sbuf_tensor("stt", [128, NT, 6], F32))
                pt = [E1(nc.psum_tensor(f"pt{k}", [128, 1024], BF16)) for k in range(2)]
                s1xs = E1(nc.semaphore("s1xs"))
                s1xa = E1(nc.semaphore("s1xa"))
                s1rd = E1(nc.semaphore("s1rd"))
                s1rdo = E1(nc.semaphore("s1rdo"))
                s1sq = E1(nc.semaphore("s1sq"))
                s1var = E1(nc.semaphore("s1var"))
                s1sqrt = E1(nc.semaphore("s1sqrt"))
                s1st2 = E1(nc.semaphore("s1st2"))
                s1xn = E1(nc.semaphore("s1xn"))
                s1pet = E1(nc.semaphore("s1pet"))
                s1ev = E1(nc.semaphore("s1ev"))

                if dbg:
                    st_dbg = nc.dram_tensor("st_dbg", [128, NT * 9], F32,
                                            kind="ExternalOutput").ap()

                def xload_wait(eng, i, sweep):
                    prev = i - 3
                    if sweep == 0:
                        if prev < 0:
                            return
                        eng.wait_ge(s1rd, prev + 1)
                        eng.wait_ge(s1sq, prev + 1)
                    else:
                        if prev < 0:
                            p = max(j for j in range(NT - 3, NT) if j % 3 == i % 3)
                            eng.wait_ge(s1rd, p + 1)
                            eng.wait_ge(s1sq, p + 1)
                        else:
                            eng.wait_ge(s1xn, prev + 1)

                with nc.Block() as blk:
                    @blk.sync
                    def _(sy):
                        for sweep in range(2):
                            for i in range(0, NT, 2):
                                xload_wait(sy, i, sweep)
                                sy.dma_start(xbuf[:, i % 3, :],
                                             x[i * 128:(i + 1) * 128, :]).then_inc(s1xs, 16)
                        if dbg:
                            sy.wait_ge(s1xn, NT)
                            sy.dma_start(st_dbg[:, 0:NT], ssum[:, :]).then_inc(s1xs, 16)
                            sy.dma_start(st_dbg[:, 2 * NT:3 * NT], sqv[:, :]).then_inc(s1xs, 16)
                            sy.dma_start(st_dbg[:, 3 * NT:9 * NT],
                                         stt[:, :, :].rearrange("p a b -> p (a b)")
                                         ).then_inc(s1xs, 16)
                            sy.wait_ge(s1xs, 16 * (2 * (NT // 2) + 3))

                    @blk.vector
                    def _(v):
                        for i in range(NT):
                            if i % 2 == 0:
                                v.wait_ge(s1xs, 16 * (i // 2 + 1))
                            else:
                                v.wait_ge(s1xa, 16 * (i // 2 + 1))
                            v.reduce_sum(ssum[:, i:i + 1], xbuf[:, i % 3, :],
                                         axis=AX.X).then_inc(s1rd, 1)
                        v.wait_ge(s1sq, NT)
                        sm = lambda j: ssum[:, j:j + 1]
                        for j in range(NT):
                            v.tensor_scalar(stt[:, j, 0:1], sm(j), 1.0 / D, None, ALU.mult)
                        for j in range(NT):
                            v.tensor_mul(stt[:, j, 1:2], sm(j), sm(j))
                        for j in range(NT):
                            v.tensor_add(stt[:, j, 4:5], ssqA[:, j:j + 1], ssqB[:, j:j + 1])
                        for j in range(NT):
                            v.tensor_scalar(stt[:, j, 2:3], stt[:, j, 4:5],
                                            1.0 / D, EPS_LN, ALU.mult, ALU.add)
                        for j in range(NT):
                            v.scalar_tensor_tensor(stt[:, j, 3:4], stt[:, j, 1:2],
                                                   -1.0 / (float(D) * D), stt[:, j, 2:3],
                                                   ALU.mult, ALU.add).then_inc(s1var, 1)
                        for j in range(NT):
                            v.wait_ge(s1sqrt, j + 1)
                            ins = v.reciprocal(stt[:, j, 5:6], sqv[:, j:j + 1])
                        ins.then_inc(s1st2, 1)
                        for i in range(NT):
                            p = i % 2
                            if p == 0:
                                v.wait_ge(s1xs, 16 * (NT // 2 + i // 2 + 1))
                            else:
                                v.wait_ge(s1xa, 16 * (NT // 2 + i // 2 + 1))
                            if i >= 2:
                                v.wait_ge(s1pet, (i - 1) * 4)
                            v.tensor_scalar(xnb[:, p, :], xbuf[:, i % 3, :], stt[:, i, 0:1],
                                            stt[:, i, 5:6], ALU.subtract,
                                            ALU.mult).then_inc(s1xn, 1)

                    @blk.scalar
                    def _(s):
                        for i in range(NT):
                            if i % 2 == 1:
                                xload_wait(s, i, 0)
                                s.dma_start(xbuf[:, i % 3, :],
                                            x[i * 128:(i + 1) * 128, :]).then_inc(s1xa, 16)
                            if i % 2 == 0:
                                s.wait_ge(s1xs, 16 * (i // 2 + 1))
                            else:
                                s.wait_ge(s1xa, 16 * (i // 2 + 1))
                            s.activation(sqs[:, :], xbuf[:, i % 3, 0:D // 2], AF.Square,
                                         accum_out=ssqA[:, i:i + 1])
                            s.activation(sqs[:, :], xbuf[:, i % 3, D // 2:D], AF.Square,
                                         accum_out=ssqB[:, i:i + 1]).then_inc(s1sq, 1)
                        for j in range(NT):
                            s.wait_ge(s1var, j + 1)
                            s.activation(sqv[:, j:j + 1], stt[:, j, 3:4],
                                         AF.Sqrt).then_inc(s1sqrt, 1)
                        for i in range(NT):
                            if i % 2 == 1:
                                xload_wait(s, i, 1)
                                s.dma_start(xbuf[:, i % 3, :],
                                            x[i * 128:(i + 1) * 128, :]).then_inc(s1xa, 16)
                            for c8 in range(4):
                                s.wait_ge(s1pet, i * 4 + c8 + 1)
                                s.copy(
                                    xnT[:, c8 * 8:(c8 + 1) * 8, i * 128:(i + 1) * 128],
                                    pt[c8 % 2][:, :].rearrange("p (a b) -> p a b", b=128)
                                ).then_inc(s1ev, 1)

                    @blk.tensor
                    def _(t):
                        for i in range(NT):
                            p = i % 2
                            t.wait_ge(s1xn, i + 1)
                            for c8 in range(4):
                                gidx = i * 4 + c8
                                if gidx >= 2:
                                    t.wait_ge(s1ev, gidx - 1)
                                for k in range(8):
                                    c = c8 * 8 + k
                                    ins = t.transpose(pt[c8 % 2][:, k * 128:(k + 1) * 128],
                                                      xnb[:, p, c * 128:(c + 1) * 128],
                                                      ident[:, :])
                                ins.then_inc(s1pet, 1)

            if dbg:
                xnT_d = nc.dram_tensor("xnT_dbg", [128, DC * TPC], BF16,
                                       kind="ExternalOutput").ap()
                sdbg = mid.enter_context(nc.semaphore("sdbg"))
                with nc.Block() as blk:
                    @blk.sync
                    def _(sy):
                        sy.dma_start(xnT_d, xnT[:, :, :].rearrange("p a b -> p (a b)")
                                     ).then_inc(sdbg, 16)
                        sy.wait_ge(sdbg, 16)

            with ExitStack() as st2:
                E2 = st2.enter_context
                retrT = E2(nc.sbuf_tensor("retrT", [128, H * TPC], BF16))
                wqbuf = E2(nc.sbuf_tensor("wqbuf", [128, 2, DC, 128], BF16))
                qselb = E2(nc.sbuf_tensor("qselb", [128, TPC], BF16))
                relub = E2(nc.sbuf_tensor("relub", [128, TPC], BF16))
                expb = E2(nc.sbuf_tensor("expb", [128, TPC], BF16))
                tmin = E2(nc.sbuf_tensor("tmin", [128, TPC], F32))
                scr8 = E2(nc.sbuf_tensor("scr8", [128, TPC], F32))
                pq = [E2(nc.psum_tensor(f"pq{k}", [128, TPC], F32)) for k in range(2)]
                s2wq = E2(nc.semaphore("s2wq"))
                s2pe = E2(nc.semaphore("s2pe"))
                s2act = E2(nc.semaphore("s2act"))
                s2dve = E2(nc.semaphore("s2dve"))
                rdenb = qselb[0:1, :]
                lnrow = relub[0:1, :]

                pe_q, pe_num, pe_den, pe_bden = {}, {}, {}, {}
                cnt = 0
                for h in range(H):
                    if h >= 1:
                        pe_num[h - 1] = cnt + 1
                        cnt += 1
                    if h >= 1:
                        pe_den[h - 1] = cnt + 1
                        cnt += 1
                    cnt += 1
                    pe_q[h] = cnt
                    if h >= 1:
                        pe_bden[h - 1] = cnt + 1
                        cnt += 1
                pe_num[H - 1] = cnt + 1
                pe_den[H - 1] = cnt + 2
                pe_bden[H - 1] = cnt + 3

                with nc.Block() as blk:
                    @blk.gpsimd
                    def _(g):
                        for h in range(H):
                            if h >= 2:
                                g.wait_ge(s2pe, pe_q[h - 2])
                            g.dma_start(wqbuf[:, h % 2, :, :], wq[h]).then_inc(s2wq, 16)
                        g.wait_ge(s2wq, 16 * H)

                    @blk.tensor
                    def _(t):
                        def q_part(h, dc_lo, dc_hi):
                            P = pq[h % 2]
                            for dc in range(dc_lo, dc_hi):
                                for sub in range(4):
                                    ins = t.matmul(P[:, sub * 512:(sub + 1) * 512],
                                                   wqbuf[:, h % 2, dc, :],
                                                   xnT[:, dc, sub * 512:(sub + 1) * 512],
                                                   start=(dc == 0), stop=(dc == DC - 1))
                            return ins

                        def mm4(P, lhsT, rhs_full):
                            for sub in range(4):
                                ins = t.matmul(P[:, sub * 512:(sub + 1) * 512], lhsT,
                                               rhs_full[:, sub * 512:(sub + 1) * 512],
                                               start=True, stop=True)
                            return ins

                        def emit_num(j):
                            t.wait_ge(s2dve, 4 * j + 2)
                            t.wait_ge(s2act, 4 * j + 1)
                            mm4(pq[j % 2], M_s[:, j, :], qselb).then_inc(s2pe, 1)

                        def emit_den(j):
                            t.wait_ge(s2dve, 4 * j + 3)
                            P = pq[j % 2]
                            for sub in range(4):
                                ins = t.matmul(P[0:1, sub * 512:(sub + 1) * 512],
                                               zT[:, j:j + 1],
                                               qselb[:, sub * 512:(sub + 1) * 512],
                                               start=True, stop=True)
                            ins.then_inc(s2pe, 1)

                        def emit_bden(j):
                            t.wait_ge(s2act, 4 * j + 4)
                            mm4(pq[j % 2], ones1[:, :], rdenb).then_inc(s2pe, 1)

                        for h in range(H):
                            t.wait_ge(s2wq, 16 * (h + 1))
                            if h >= 2:
                                t.wait_ge(s2dve, 4 * (h - 2) + 4)
                            q_part(h, 0, 16)
                            if h >= 1:
                                emit_num(h - 1)
                            q_part(h, 16, 28)
                            if h >= 1:
                                emit_den(h - 1)
                            q_part(h, 28, DC).then_inc(s2pe, 1)
                            if h >= 1:
                                emit_bden(h - 1)
                        emit_num(H - 1)
                        emit_den(H - 1)
                        emit_bden(H - 1)

                    @blk.scalar
                    def _(s):
                        for h in range(H):
                            s.wait_ge(s2pe, pe_q[h])
                            if h >= 1:
                                s.wait_ge(s2dve, 4 * (h - 1) + 2)
                            s.activation(relub[:, :], pq[h % 2][:, :], AF.Relu,
                                         bias=bqT[:, h:h + 1]).then_inc(s2act, 1)
                            s.wait_ge(s2dve, 4 * h + 1)
                            s.activation(expb[:, :], tmin[:, :], AF.Exp).then_inc(s2act, 1)
                            s.wait_ge(s2pe, pe_den[h])
                            s.activation(lnrow, pq[h % 2][0:1, :], AF.Ln,
                                         bias=epsb[0:1, 0:1]).then_inc(s2act, 1)
                            s.activation(rdenb, lnrow, AF.Exp, scale=-1.0).then_inc(s2act, 1)
                        s.wait_ge(s2dve, 4 * H)
                        s.dma_start(retr, retrT[:, :]).then_inc(s2act, 16)
                        s.wait_ge(s2act, 4 * H + 16)

                    @blk.vector
                    def _(v):
                        for h in range(H):
                            P = pq[h % 2]
                            v.wait_ge(s2pe, pe_q[h])
                            v.wait_ge(s2act, 4 * h + 1)
                            v.tensor_scalar(tmin[:, :], P[:, :], bqT[:, h:h + 1], 0.0,
                                            ALU.add, ALU.min).then_inc(s2dve, 1)
                            v.wait_ge(s2act, 4 * h + 2)
                            if h >= 1:
                                v.wait_ge(s2pe, pe_bden[h - 1])
                            v.tensor_add(qselb[:, :], expb[:, :], relub[:, :]).then_inc(s2dve, 1)
                            v.wait_ge(s2pe, pe_num[h])
                            v.tensor_copy(scr8[:, :], P[:, :]).then_inc(s2dve, 1)
                            v.wait_ge(s2pe, pe_bden[h])
                            v.tensor_mul(retrT[:, h * TPC:(h + 1) * TPC],
                                         scr8[:, :], P[:, :]).then_inc(s2dve, 1)

            NJ = 8 * H
            with ExitStack() as st3:
                E3 = st3.enter_context
                wgbuf = E3(nc.sbuf_tensor("wgbuf", [128, 2, DC, 128], BF16))
                wubuf = E3(nc.sbuf_tensor("wubuf", [128, 2, DC, 128], BF16))
                silb = E3(nc.sbuf_tensor("silb", [128, TPC], F32))
                hbuf = E3(nc.sbuf_tensor("hbuf", [128, 2, TPC], BF16))
                wo4 = E3(nc.sbuf_tensor("wo4", [128, 2, 4, 512], F32))
                woeb = E3(nc.sbuf_tensor("woeb", [128, 2, 512], BF16))
                pg = E3(nc.psum_tensor("pg", [128, TPC], F32))
                pu = E3(nc.psum_tensor("pu", [128, TPC], F32))
                s3wld = E3(nc.semaphore("s3wld"))
                s3pe = E3(nc.semaphore("s3pe"))
                s3act = E3(nc.semaphore("s3act"))
                s3dve = E3(nc.semaphore("s3dve"))
                s3st = E3(nc.semaphore("s3st"))
                s15ld = E3(nc.semaphore("s15ld"))
                s15dve = E3(nc.semaphore("s15dve"))
                s15st = E3(nc.semaphore("s15st"))

                def wo_piece(j):
                    return j // 8, j % 8

                with nc.Block() as blk:
                    @blk.sync
                    def _(sy):
                        nj = 0
                        for fc in range(FC + 30):
                            while nj < NJ and nj * FC < (fc + 1) * NJ:
                                j = nj
                                h, qt = wo_piece(j)
                                if j >= 2:
                                    sy.wait_ge(s15dve, j - 1)
                                for g in range(4):
                                    sy.dma_start(wo4[:, j % 2, g, :],
                                                 wo[h, g, :, qt * 512:(qt + 1) * 512]
                                                 ).then_inc(s15ld, 16)
                                if j >= 2:
                                    jp = j - 2
                                    hp, qp = wo_piece(jp)
                                    sy.wait_ge(s15dve, jp + 1)
                                    sy.dma_start(woe[hp, :, qp * 512:(qp + 1) * 512],
                                                 woeb[:, jp % 2, :]).then_inc(s15st, 16)
                                nj += 1
                        for j in (NJ - 2, NJ - 1):
                            h, qt = wo_piece(j)
                            sy.wait_ge(s15dve, j + 1)
                            sy.dma_start(woe[h, :, qt * 512:(qt + 1) * 512],
                                         woeb[:, j % 2, :]).then_inc(s15st, 16)
                        sy.wait_ge(s15st, 16 * NJ)

                    @blk.gpsimd
                    def _(g):
                        for fc in range(FC):
                            if fc >= 2:
                                g.wait_ge(s3pe, 2 * (fc - 1))
                            g.dma_start(wgbuf[:, fc % 2, :, :], wg[fc]).then_inc(s3wld, 16)
                            g.dma_start(wubuf[:, fc % 2, :, :], wu[fc]).then_inc(s3wld, 16)

                    @blk.tensor
                    def _(t):
                        for fc in range(FC):
                            t.wait_ge(s3wld, 32 * (fc + 1))
                            if fc >= 1:
                                t.wait_ge(s3act, fc)
                            for dc in range(DC):
                                for sub in range(4):
                                    ins = t.matmul(pg[:, sub * 512:(sub + 1) * 512],
                                                   wgbuf[:, fc % 2, dc, :],
                                                   xnT[:, dc, sub * 512:(sub + 1) * 512],
                                                   start=(dc == 0), stop=(dc == DC - 1))
                            ins.then_inc(s3pe, 1)
                            if fc >= 1:
                                t.wait_ge(s3dve, fc)
                            for dc in range(DC):
                                for sub in range(4):
                                    ins = t.matmul(pu[:, sub * 512:(sub + 1) * 512],
                                                   wubuf[:, fc % 2, dc, :],
                                                   xnT[:, dc, sub * 512:(sub + 1) * 512],
                                                   start=(dc == 0), stop=(dc == DC - 1))
                            ins.then_inc(s3pe, 1)

                    @blk.scalar
                    def _(s):
                        for fc in range(FC):
                            s.wait_ge(s3pe, 2 * fc + 1)
                            if fc >= 1:
                                s.wait_ge(s3dve, fc)
                            s.activation(silb[:, :], pg[:, :], AF.Silu).then_inc(s3act, 1)
                            s.wait_ge(s3dve, fc + 1)
                            s.dma_start(hT[fc], hbuf[:, fc % 2, :]).then_inc(s3st, 16)
                        s.wait_ge(s3st, 16 * FC)

                    @blk.vector
                    def _(v):
                        nj = 0
                        for fc in range(FC):
                            v.wait_ge(s3act, fc + 1)
                            v.wait_ge(s3pe, 2 * fc + 2)
                            if fc >= 2:
                                v.wait_ge(s3st, 16 * (fc - 1))
                            v.tensor_mul(hbuf[:, fc % 2, :], silb[:, :], pu[:, :]).then_inc(s3dve, 1)
                            while nj < NJ and nj * FC < (fc + 1) * NJ:
                                j = nj
                                q = j % 2
                                v.wait_ge(s15ld, 64 * (j + 1))
                                v.tensor_add(wo4[:, q, 0, :], wo4[:, q, 0, :], wo4[:, q, 1, :])
                                v.tensor_add(wo4[:, q, 2, :], wo4[:, q, 2, :], wo4[:, q, 3, :])
                                if j >= 2:
                                    v.wait_ge(s15st, 16 * (j - 1))
                                v.tensor_add(woeb[:, q, :], wo4[:, q, 0, :],
                                             wo4[:, q, 2, :]).then_inc(s15dve, 1)
                                nj += 1
                        assert nj == NJ

        with ExitStack() as st4:
            E4 = st4.enter_context
            retr_s = E4(nc.sbuf_tensor("retr_s4", [128, H * TPC], BF16))
            acc = E4(nc.sbuf_tensor("acc", [128, NT * SLAB], F32))
            woeff = E4(nc.sbuf_tensor("woeff", [128, H, SLAB], BF16))
            hbuf4 = E4(nc.sbuf_tensor("hbuf4", [128, NSLOT, TPC], BF16))
            wdbuf4 = E4(nc.sbuf_tensor("wdbuf4", [128, NSLOT, SLAB], BF16))
            pp = [E4(nc.psum_tensor(f"pp{k}", [128, SLAB], F32)) for k in range(4)]
            s4hld = E4(nc.semaphore("s4hld"))
            s4wld = E4(nc.semaphore("s4wld"))
            s4pe = E4(nc.semaphore("s4pe"))
            s4fl = E4(nc.semaphore("s4fl"))
            s4wo = E4(nc.semaphore("s4wo"))
            s4acc = E4(nc.semaphore("s4acc"))
            s4out = E4(nc.semaphore("s4out"))
            s4outb = E4(nc.semaphore("s4outb"))
            s4retr = E4(nc.semaphore("s4retr"))

            def chunk_group(c):
                return 0 if c < FBS[0] else (c - FBS[0]) // 7 + 1

            def slot_wait_val(c_glob):
                p = c_glob - NSLOT
                if p < 0:
                    return 0
                sp, cp = p // FC, p % FC
                return (sp * NGRP + chunk_group(cp) + 1) * 16

            CHK_AHEAD = sum(FBS[:2])

            with nc.Block() as blk:
                @blk.sync
                def _(sy):
                    def load_chunks(s, c_lo, c_hi):
                        for c in range(c_lo, c_hi):
                            c_glob = s * FC + c
                            w = slot_wait_val(c_glob)
                            if w > 0:
                                sy.wait_ge(s4pe, w)
                            sy.dma_start(hbuf4[:, c_glob % NSLOT, :],
                                         hT[c]).then_inc(s4hld, 16)

                    for s in range(NSLAB):
                        load_chunks(s, 0, (FBS[0] + 2) if s == 0 else CHK_AHEAD)
                        for t in range(NT):
                            if s > 0:
                                sy.wait_ge(s4out, 16 * ((s - 1) * NT + t + 1))
                            sy.dma_start(acc[:, t * SLAB:(t + 1) * SLAB],
                                         x[t * 128:(t + 1) * 128, s * SLAB:(s + 1) * SLAB]
                                         ).then_inc(s4acc, 16)
                        load_chunks(s, (FBS[0] + 2) if s == 0 else CHK_AHEAD, FC)
                    sy.wait_ge(s4out, 16 * NSLAB * NT)

                @blk.gpsimd
                def _(g):
                    for s in range(NSLAB):
                        for c in range(FC):
                            c_glob = s * FC + c
                            w = slot_wait_val(c_glob)
                            if w > 0:
                                g.wait_ge(s4pe, w)
                            g.dma_start(wdbuf4[:, c_glob % NSLOT, :],
                                        wd[c][:, s * SLAB:(s + 1) * SLAB]).then_inc(s4wld, 16)

                @blk.scalar
                def _(sc):
                    woe_p = woe.rearrange("h p o -> p h o")
                    sc.dma_start(retr_s[:, :], retr).then_inc(s4retr, 16)
                    sc.dma_start(woeff[:, :, :], woe_p[:, :, 0:SLAB]).then_inc(s4wo, 16)
                    for s in range(NSLAB):
                        if s + 1 < NSLAB:
                            sc.wait_ge(s4pe, (s * NGRP + 1) * 16)
                            sc.dma_start(woeff[:, :, :],
                                         woe_p[:, :, (s + 1) * SLAB:(s + 2) * SLAB]
                                         ).then_inc(s4wo, 16)
                        for t in range(NT):
                            sc.wait_ge(s4fl, (s * NGRP + NGRP - 1) * 16 + t + 1)
                            sc.dma_start(out[t * 128:(t + 1) * 128, s * SLAB:(s + 1) * SLAB],
                                         acc[:, t * SLAB:(t + 1) * SLAB]).then_inc(s4out, 16)
                    sc.wait_ge(s4out, 16 * NSLAB * NT)

                @blk.tensor
                def _(t_):
                    for s in range(NSLAB):
                        for g, fb in enumerate(FBS):
                            chunks = [sum(FBS[:g]) + k for k in range(fb)]
                            cum = s * FC + chunks[-1] + 1
                            cum_sl = min(cum + 2, NSLAB * FC)
                            t_.wait_ge(s4hld, 16 * cum_sl)
                            t_.wait_ge(s4wld, 16 * cum_sl)
                            if g == 0:
                                t_.wait_ge(s4wo, 16 * (s + 1))
                                t_.wait_ge(s4retr, 16)
                                t_.wait_ge(s4acc, 16 * min(s * NT + 2, NSLAB * NT))
                            for t in range(NT):
                                kg = (s * NGRP + g) * 16 + t
                                if kg >= 4:
                                    t_.wait_ge(s4fl, kg - 3)
                                ps = pp[kg % 4]
                                per_n = (8 if g == 0 else 0) + fb
                                seen = [0, 0]
                                if g == 0:
                                    for h in range(H):
                                        for n in range(2):
                                            seen[n] += 1
                                            ins = t_.matmul(
                                                ps[:, n * 512:(n + 1) * 512],
                                                retr_s[:, h * TPC + t * 128: h * TPC + (t + 1) * 128],
                                                woeff[:, h, n * 512:(n + 1) * 512],
                                                start=(seen[n] == 1), stop=(seen[n] == per_n))
                                for c in chunks:
                                    sl = (s * FC + c) % NSLOT
                                    for n in range(2):
                                        seen[n] += 1
                                        ins = t_.matmul(
                                            ps[:, n * 512:(n + 1) * 512],
                                            hbuf4[:, sl, t * 128:(t + 1) * 128],
                                            wdbuf4[:, sl, n * 512:(n + 1) * 512],
                                            start=(seen[n] == 1), stop=(seen[n] == per_n))
                                ins.then_inc(s4pe, 1)

                @blk.vector
                def _(v):
                    for s in range(NSLAB):
                        for g in range(NGRP):
                            for t in range(NT):
                                kg = (s * NGRP + g) * 16 + t
                                v.wait_ge(s4pe, kg + 1)
                                if g == 0:
                                    v.wait_ge(s4acc, 16 * min(s * NT + t + 3, (s + 1) * NT))
                                v.tensor_tensor(acc[:, t * SLAB:(t + 1) * SLAB],
                                                acc[:, t * SLAB:(t + 1) * SLAB],
                                                pp[kg % 4][:, :], ALU.add).then_inc(s4fl, 1)
    return nc


def _prep_inputs(L0_attn_hidden_states, L0_attn_Wq, L0_attn_bq, L0_attn_Wo,
                 L0_attn_M, L0_attn_z, L0_ffn_W_gate, L0_ffn_W_up, L0_ffn_W_down):
    f32 = lambda a: np.asarray(a, dtype=np.float32)
    hs = f32(L0_attn_hidden_states).reshape(-1, D)
    Wq = f32(L0_attn_Wq)
    bqv = f32(L0_attn_bq)
    Wo = f32(L0_attn_Wo)
    M = np.ascontiguousarray(f32(L0_attn_M))
    z = f32(L0_attn_z)
    Wg = f32(L0_ffn_W_gate)
    Wu = f32(L0_ffn_W_up)
    Wd = f32(L0_ffn_W_down)

    A = Wq.reshape(32, 128, D)[::4]
    wq_t = np.ascontiguousarray(A.reshape(H, 128, DC, 128).transpose(0, 3, 2, 1))
    bq_t = np.ascontiguousarray(bqv.reshape(32, 128)[::4].T)
    z_t = np.ascontiguousarray(z.T)
    wo_t = np.ascontiguousarray(Wo.T).reshape(H, 4, 128, D)
    wg_t = np.ascontiguousarray(Wg.reshape(FC, 128, DC, 128).transpose(0, 3, 2, 1))
    wu_t = np.ascontiguousarray(Wu.reshape(FC, 128, DC, 128).transpose(0, 3, 2, 1))
    wd_t = np.ascontiguousarray(Wd.T).reshape(FC, 128, D)

    shared = {"wq": wq_t, "bq": bq_t, "zt": z_t, "mm": M, "wo": wo_t,
              "wg": wg_t, "wu": wu_t, "wd": wd_t}
    shards = [np.ascontiguousarray(hs[c * TPC:(c + 1) * TPC]) for c in range(NCORES)]
    return shared, shards


def kernel(L0_attn_hidden_states, L0_attn_Wq, L0_attn_bq, L0_attn_Wk, L0_attn_bk,
           L0_attn_Wv, L0_attn_bv, L0_attn_Wo, L0_attn_M, L0_attn_z,
           L0_ffn_W_gate, L0_ffn_W_up, L0_ffn_W_down):
    shared, shards = _prep_inputs(L0_attn_hidden_states, L0_attn_Wq, L0_attn_bq,
                                  L0_attn_Wo, L0_attn_M, L0_attn_z,
                                  L0_ffn_W_gate, L0_ffn_W_up, L0_ffn_W_down)
    nc = build_nc()
    in_maps = [dict(shared, x=shards[c]) for c in range(NCORES)]
    res = run_bass_kernel_spmd(nc, in_maps, core_ids=list(range(NCORES)))
    outp = np.concatenate([res.results[c]["out"] for c in range(NCORES)], axis=0)
    return outp.reshape(2, 8192, D)


# version 13
# speedup vs baseline: 1.0379x; 1.0379x over previous
import numpy as np
import concourse.bass as bass
import concourse.mybir as mybir
from concourse.bass_utils import run_bass_kernel_spmd
from contextlib import ExitStack

F32 = mybir.dt.float32
BF16 = mybir.dt.bfloat16
AF = mybir.ActivationFunctionType
ALU = mybir.AluOpType
AX = mybir.AxisListType

NCORES = 8
TPC = 2048
NT = TPC // 128
D = 4096
DC = D // 128
DFF = 11008
FC = DFF // 128
H = 8
SLAB = 1024
NSLAB = D // SLAB
FBS = [2] + [7] * 12
NGRP = len(FBS)
NSLOT = 15
OPT_P4B = True
OPT_P4E = True
EPS_LN = 1e-5
EPS_DEN = 1e-6


def build_nc(dbg=False):
    nc = bass.Bass("TRN2", target_bir_lowering=False)
    x = nc.dram_tensor("x", [TPC, D], F32, kind="ExternalInput").ap()
    wq = nc.dram_tensor("wq", [H, 128, DC, 128], F32, kind="ExternalInput").ap()
    bq = nc.dram_tensor("bq", [128, H], F32, kind="ExternalInput").ap()
    zt = nc.dram_tensor("zt", [128, H], F32, kind="ExternalInput").ap()
    mm = nc.dram_tensor("mm", [H, 128, 128], F32, kind="ExternalInput").ap()
    wo = nc.dram_tensor("wo", [H, 4, 128, D], F32, kind="ExternalInput").ap()
    wg = nc.dram_tensor("wg", [FC, 128, DC, 128], F32, kind="ExternalInput").ap()
    wu = nc.dram_tensor("wu", [FC, 128, DC, 128], F32, kind="ExternalInput").ap()
    wd = nc.dram_tensor("wd", [FC, 128, D], F32, kind="ExternalInput").ap()
    out = nc.dram_tensor("out", [TPC, D], F32, kind="ExternalOutput").ap()

    scratch_kind = "ExternalOutput" if dbg else "Internal"
    hT = nc.dram_tensor("hT_s", [FC, 128, TPC], BF16, kind=scratch_kind).ap()
    woe = nc.dram_tensor("woe_s", [H, 128, D], BF16, kind=scratch_kind).ap()
    retr = nc.dram_tensor("retr_s", [128, H * TPC], BF16, kind=scratch_kind).ap()
    wdbf = nc.dram_tensor("wdbf_s", [FC, 128, D], BF16, kind="Internal").ap()

    with ExitStack() as top:
        E = top.enter_context
        ident = E(nc.sbuf_tensor("ident", [128, 128], BF16))
        ones1 = E(nc.sbuf_tensor("ones1", [1, 128], BF16))
        bqT = E(nc.sbuf_tensor("bqT", [128, H], F32))
        zT = E(nc.sbuf_tensor("zT", [128, H], BF16))
        M_s = E(nc.sbuf_tensor("M_s", [128, H, 128], BF16))
        epsb = E(nc.sbuf_tensor("epsb", [1, 1], F32))
        s0 = E(nc.semaphore("s0"))

        with nc.Block() as blk:
            @blk.gpsimd
            def _(g):
                g.dma_start(zT[:, :], zt).then_inc(s0, 16)
                g.dma_start(M_s[:, :, :], mm.rearrange("h d e -> d h e")).then_inc(s0, 16)
                g.memset(ident[:, :], 0.0)
                g.affine_select(out=ident[:, :], in_=ident[:, :],
                                compare_op=ALU.not_equal, fill=1.0, base=0,
                                pattern=[[-1, 128]], channel_multiplier=1)
                g.memset(ones1[:, :], 1.0)
                g.memset(epsb[:, :], EPS_DEN)
                g.wait_ge(s0, 48)

            @blk.sync
            def _(sy):
                sy.dma_start(bqT[:, :], bq).then_inc(s0, 16)

        with ExitStack() as mid:
            xnT = mid.enter_context(nc.sbuf_tensor("xnT", [128, DC, TPC], BF16))

            with ExitStack() as st1:
                E1 = st1.enter_context
                xbuf = E1(nc.sbuf_tensor("xbuf", [128, 3, D], F32))
                sqs = E1(nc.sbuf_tensor("sqs", [128, D // 2], F32))
                xnb = E1(nc.sbuf_tensor("xnb", [128, 2, D], BF16))
                ssum = E1(nc.sbuf_tensor("ssum", [128, NT], F32))
                ssqA = E1(nc.sbuf_tensor("ssqA", [128, NT], F32))
                ssqB = E1(nc.sbuf_tensor("ssqB", [128, NT], F32))
                sqv = E1(nc.sbuf_tensor("sqv", [128, NT], F32))
                stt = E1(nc.sbuf_tensor("stt", [128, NT, 6], F32))
                pt = [E1(nc.psum_tensor(f"pt{k}", [128, 1024], BF16)) for k in range(2)]
                s1xs = E1(nc.semaphore("s1xs"))
                s1xa = E1(nc.semaphore("s1xa"))
                s1rd = E1(nc.semaphore("s1rd"))
                s1rdo = E1(nc.semaphore("s1rdo"))
                s1sq = E1(nc.semaphore("s1sq"))
                s1var = E1(nc.semaphore("s1var"))
                s1sqrt = E1(nc.semaphore("s1sqrt"))
                s1st2 = E1(nc.semaphore("s1st2"))
                s1xn = E1(nc.semaphore("s1xn"))
                s1pet = E1(nc.semaphore("s1pet"))
                s1ev = E1(nc.semaphore("s1ev"))

                if dbg:
                    st_dbg = nc.dram_tensor("st_dbg", [128, NT * 9], F32,
                                            kind="ExternalOutput").ap()

                def xload_wait(eng, i, sweep):
                    prev = i - 3
                    if sweep == 0:
                        if prev < 0:
                            return
                        eng.wait_ge(s1rd, prev + 1)
                        eng.wait_ge(s1sq, prev + 1)
                    else:
                        if prev < 0:
                            p = max(j for j in range(NT - 3, NT) if j % 3 == i % 3)
                            eng.wait_ge(s1rd, p + 1)
                            eng.wait_ge(s1sq, p + 1)
                        else:
                            eng.wait_ge(s1xn, prev + 1)

                with nc.Block() as blk:
                    @blk.sync
                    def _(sy):
                        for sweep in range(2):
                            for i in range(0, NT, 2):
                                xload_wait(sy, i, sweep)
                                sy.dma_start(xbuf[:, i % 3, :],
                                             x[i * 128:(i + 1) * 128, :]).then_inc(s1xs, 16)
                        if dbg:
                            sy.wait_ge(s1xn, NT)
                            sy.dma_start(st_dbg[:, 0:NT], ssum[:, :]).then_inc(s1xs, 16)
                            sy.dma_start(st_dbg[:, 2 * NT:3 * NT], sqv[:, :]).then_inc(s1xs, 16)
                            sy.dma_start(st_dbg[:, 3 * NT:9 * NT],
                                         stt[:, :, :].rearrange("p a b -> p (a b)")
                                         ).then_inc(s1xs, 16)
                            sy.wait_ge(s1xs, 16 * (2 * (NT // 2) + 3))

                    @blk.vector
                    def _(v):
                        for i in range(NT):
                            if i % 2 == 0:
                                v.wait_ge(s1xs, 16 * (i // 2 + 1))
                            else:
                                v.wait_ge(s1xa, 16 * (i // 2 + 1))
                            v.reduce_sum(ssum[:, i:i + 1], xbuf[:, i % 3, :],
                                         axis=AX.X).then_inc(s1rd, 1)
                        v.wait_ge(s1sq, NT)
                        sm = lambda j: ssum[:, j:j + 1]
                        for j in range(NT):
                            v.tensor_scalar(stt[:, j, 0:1], sm(j), 1.0 / D, None, ALU.mult)
                        for j in range(NT):
                            v.tensor_mul(stt[:, j, 1:2], sm(j), sm(j))
                        for j in range(NT):
                            v.tensor_add(stt[:, j, 4:5], ssqA[:, j:j + 1], ssqB[:, j:j + 1])
                        for j in range(NT):
                            v.tensor_scalar(stt[:, j, 2:3], stt[:, j, 4:5],
                                            1.0 / D, EPS_LN, ALU.mult, ALU.add)
                        for j in range(NT):
                            v.scalar_tensor_tensor(stt[:, j, 3:4], stt[:, j, 1:2],
                                                   -1.0 / (float(D) * D), stt[:, j, 2:3],
                                                   ALU.mult, ALU.add).then_inc(s1var, 1)
                        for j in range(NT):
                            v.wait_ge(s1sqrt, j + 1)
                            ins = v.reciprocal(stt[:, j, 5:6], sqv[:, j:j + 1])
                        ins.then_inc(s1st2, 1)
                        for i in range(NT):
                            p = i % 2
                            if p == 0:
                                v.wait_ge(s1xs, 16 * (NT // 2 + i // 2 + 1))
                            else:
                                v.wait_ge(s1xa, 16 * (NT // 2 + i // 2 + 1))
                            if i >= 2:
                                v.wait_ge(s1pet, (i - 1) * 4)
                            v.tensor_scalar(xnb[:, p, :], xbuf[:, i % 3, :], stt[:, i, 0:1],
                                            stt[:, i, 5:6], ALU.subtract,
                                            ALU.mult).then_inc(s1xn, 1)

                    @blk.scalar
                    def _(s):
                        def odd_load(j, sweep):
                            xload_wait(s, j, sweep)
                            s.dma_start(xbuf[:, j % 3, :],
                                        x[j * 128:(j + 1) * 128, :]).then_inc(s1xa, 16)

                        odd_load(1, 0)
                        for i in range(NT):
                            if i % 2 == 0:
                                s.wait_ge(s1xs, 16 * (i // 2 + 1))
                            else:
                                s.wait_ge(s1xa, 16 * (i // 2 + 1))
                            s.activation(sqs[:, :], xbuf[:, i % 3, 0:D // 2], AF.Square,
                                         accum_out=ssqA[:, i:i + 1])
                            s.activation(sqs[:, :], xbuf[:, i % 3, D // 2:D], AF.Square,
                                         accum_out=ssqB[:, i:i + 1]).then_inc(s1sq, 1)
                            if i + 3 < NT and (i + 3) % 2 == 1:
                                odd_load(i + 3, 0)
                        for j in range(NT):
                            s.wait_ge(s1var, j + 1)
                            s.activation(sqv[:, j:j + 1], stt[:, j, 3:4],
                                         AF.Sqrt).then_inc(s1sqrt, 1)
                        odd_load(1, 1)
                        for i in range(NT):
                            if i + 3 < NT and (i + 3) % 2 == 1:
                                odd_load(i + 3, 1)
                            for c8 in range(4):
                                s.wait_ge(s1pet, i * 4 + c8 + 1)
                                s.copy(
                                    xnT[:, c8 * 8:(c8 + 1) * 8, i * 128:(i + 1) * 128],
                                    pt[c8 % 2][:, :].rearrange("p (a b) -> p a b", b=128)
                                ).then_inc(s1ev, 1)

                    @blk.tensor
                    def _(t):
                        for i in range(NT):
                            p = i % 2
                            t.wait_ge(s1xn, i + 1)
                            for c8 in range(4):
                                gidx = i * 4 + c8
                                if gidx >= 2:
                                    t.wait_ge(s1ev, gidx - 1)
                                for k in range(8):
                                    c = c8 * 8 + k
                                    ins = t.transpose(pt[c8 % 2][:, k * 128:(k + 1) * 128],
                                                      xnb[:, p, c * 128:(c + 1) * 128],
                                                      ident[:, :])
                                ins.then_inc(s1pet, 1)

            if dbg:
                xnT_d = nc.dram_tensor("xnT_dbg", [128, DC * TPC], BF16,
                                       kind="ExternalOutput").ap()
                sdbg = mid.enter_context(nc.semaphore("sdbg"))
                with nc.Block() as blk:
                    @blk.sync
                    def _(sy):
                        sy.dma_start(xnT_d, xnT[:, :, :].rearrange("p a b -> p (a b)")
                                     ).then_inc(sdbg, 16)
                        sy.wait_ge(sdbg, 16)

            with ExitStack() as st2:
                E2 = st2.enter_context
                retrT = E2(nc.sbuf_tensor("retrT", [128, H * TPC], BF16))
                wqbuf = E2(nc.sbuf_tensor("wqbuf", [128, 2, DC, 128], BF16))
                qselb = E2(nc.sbuf_tensor("qselb", [128, TPC], BF16))
                relub = E2(nc.sbuf_tensor("relub", [128, TPC], BF16))
                expb = E2(nc.sbuf_tensor("expb", [128, TPC], BF16))
                tmin = E2(nc.sbuf_tensor("tmin", [128, TPC], F32))
                scr8 = E2(nc.sbuf_tensor("scr8", [128, TPC], F32))
                pq = [E2(nc.psum_tensor(f"pq{k}", [128, TPC], F32)) for k in range(2)]
                s2wq = E2(nc.semaphore("s2wq"))
                s2pe = E2(nc.semaphore("s2pe"))
                s2act = E2(nc.semaphore("s2act"))
                s2dve = E2(nc.semaphore("s2dve"))
                rdenb = qselb[0:1, :]
                lnrow = relub[0:1, :]

                pe_q, pe_num, pe_den, pe_bden = {}, {}, {}, {}
                cnt = 0
                for h in range(H):
                    if h >= 1:
                        pe_num[h - 1] = cnt + 1
                        cnt += 1
                    if h >= 1:
                        pe_den[h - 1] = cnt + 1
                        cnt += 1
                    cnt += 1
                    pe_q[h] = cnt
                    if h >= 1:
                        pe_bden[h - 1] = cnt + 1
                        cnt += 1
                pe_num[H - 1] = cnt + 1
                pe_den[H - 1] = cnt + 2
                pe_bden[H - 1] = cnt + 3

                with nc.Block() as blk:
                    @blk.gpsimd
                    def _(g):
                        for h in range(H):
                            if h >= 2:
                                g.wait_ge(s2pe, pe_q[h - 2])
                            g.dma_start(wqbuf[:, h % 2, :, :], wq[h]).then_inc(s2wq, 16)
                        g.wait_ge(s2wq, 16 * H)

                    @blk.tensor
                    def _(t):
                        def q_part(h, dc_lo, dc_hi):
                            P = pq[h % 2]
                            for dc in range(dc_lo, dc_hi):
                                for sub in range(4):
                                    ins = t.matmul(P[:, sub * 512:(sub + 1) * 512],
                                                   wqbuf[:, h % 2, dc, :],
                                                   xnT[:, dc, sub * 512:(sub + 1) * 512],
                                                   start=(dc == 0), stop=(dc == DC - 1))
                            return ins

                        def mm4(P, lhsT, rhs_full):
                            for sub in range(4):
                                ins = t.matmul(P[:, sub * 512:(sub + 1) * 512], lhsT,
                                               rhs_full[:, sub * 512:(sub + 1) * 512],
                                               start=True, stop=True)
                            return ins

                        def emit_num(j):
                            t.wait_ge(s2dve, 4 * j + 2)
                            t.wait_ge(s2act, 4 * j + 1)
                            mm4(pq[j % 2], M_s[:, j, :], qselb).then_inc(s2pe, 1)

                        def emit_den(j):
                            t.wait_ge(s2dve, 4 * j + 3)
                            P = pq[j % 2]
                            for sub in range(4):
                                ins = t.matmul(P[0:1, sub * 512:(sub + 1) * 512],
                                               zT[:, j:j + 1],
                                               qselb[:, sub * 512:(sub + 1) * 512],
                                               start=True, stop=True)
                            ins.then_inc(s2pe, 1)

                        def emit_bden(j):
                            t.wait_ge(s2act, 4 * j + 4)
                            mm4(pq[j % 2], ones1[:, :], rdenb).then_inc(s2pe, 1)

                        for h in range(H):
                            t.wait_ge(s2wq, 16 * (h + 1))
                            if h >= 2:
                                t.wait_ge(s2dve, 4 * (h - 2) + 4)
                            q_part(h, 0, 16)
                            if h >= 1:
                                emit_num(h - 1)
                            q_part(h, 16, 28)
                            if h >= 1:
                                emit_den(h - 1)
                            q_part(h, 28, DC).then_inc(s2pe, 1)
                            if h >= 1:
                                emit_bden(h - 1)
                        emit_num(H - 1)
                        emit_den(H - 1)
                        emit_bden(H - 1)

                    @blk.scalar
                    def _(s):
                        for h in range(H):
                            s.wait_ge(s2pe, pe_q[h])
                            if h >= 1:
                                s.wait_ge(s2dve, 4 * (h - 1) + 2)
                            s.activation(relub[:, :], pq[h % 2][:, :], AF.Relu,
                                         bias=bqT[:, h:h + 1]).then_inc(s2act, 1)
                            s.wait_ge(s2dve, 4 * h + 1)
                            s.activation(expb[:, :], tmin[:, :], AF.Exp).then_inc(s2act, 1)
                            s.wait_ge(s2pe, pe_den[h])
                            s.activation(lnrow, pq[h % 2][0:1, :], AF.Ln,
                                         bias=epsb[0:1, 0:1]).then_inc(s2act, 1)
                            s.activation(rdenb, lnrow, AF.Exp, scale=-1.0).then_inc(s2act, 1)
                        s.wait_ge(s2dve, 4 * H)
                        s.dma_start(retr, retrT[:, :]).then_inc(s2act, 16)
                        s.wait_ge(s2act, 4 * H + 16)

                    @blk.vector
                    def _(v):
                        for h in range(H):
                            P = pq[h % 2]
                            v.wait_ge(s2pe, pe_q[h])
                            v.wait_ge(s2act, 4 * h + 1)
                            v.tensor_scalar(tmin[:, :], P[:, :], bqT[:, h:h + 1], 0.0,
                                            ALU.add, ALU.min).then_inc(s2dve, 1)
                            v.wait_ge(s2act, 4 * h + 2)
                            if h >= 1:
                                v.wait_ge(s2pe, pe_bden[h - 1])
                            v.tensor_add(qselb[:, :], expb[:, :], relub[:, :]).then_inc(s2dve, 1)
                            v.wait_ge(s2pe, pe_num[h])
                            v.tensor_copy(scr8[:, :], P[:, :]).then_inc(s2dve, 1)
                            v.wait_ge(s2pe, pe_bden[h])
                            v.tensor_mul(retrT[:, h * TPC:(h + 1) * TPC],
                                         scr8[:, :], P[:, :]).then_inc(s2dve, 1)

            NJ = 8 * H
            with ExitStack() as st3:
                E3 = st3.enter_context
                wgbuf = E3(nc.sbuf_tensor("wgbuf", [128, 2, DC, 128], BF16))
                wubuf = E3(nc.sbuf_tensor("wubuf", [128, 2, DC, 128], BF16))
                silb = E3(nc.sbuf_tensor("silb", [128, TPC], F32))
                hbuf = E3(nc.sbuf_tensor("hbuf", [128, 2, TPC], BF16))
                wo4 = E3(nc.sbuf_tensor("wo4", [128, 2, 4, 512], F32))
                woeb = E3(nc.sbuf_tensor("woeb", [128, 2, 512], BF16))
                pg = E3(nc.psum_tensor("pg", [128, TPC], F32))
                pu = E3(nc.psum_tensor("pu", [128, TPC], F32))
                s3wld = E3(nc.semaphore("s3wld"))
                s3pe = E3(nc.semaphore("s3pe"))
                s3act = E3(nc.semaphore("s3act"))
                s3dve = E3(nc.semaphore("s3dve"))
                s3st = E3(nc.semaphore("s3st"))
                s3wd = E3(nc.semaphore("s3wd"))
                s15ld = E3(nc.semaphore("s15ld"))
                s15dve = E3(nc.semaphore("s15dve"))
                s15st = E3(nc.semaphore("s15st"))

                def wo_piece(j):
                    return j // 8, j % 8

                with nc.Block() as blk:
                    @blk.sync
                    def _(sy):
                        nj = 0
                        for fc in range(FC + 30):
                            while nj < NJ and nj * FC < (fc + 1) * NJ:
                                j = nj
                                h, qt = wo_piece(j)
                                if j >= 2:
                                    sy.wait_ge(s15dve, j - 1)
                                for g in range(4):
                                    sy.dma_start(wo4[:, j % 2, g, :],
                                                 wo[h, g, :, qt * 512:(qt + 1) * 512]
                                                 ).then_inc(s15ld, 16)
                                if j >= 2:
                                    jp = j - 2
                                    hp, qp = wo_piece(jp)
                                    sy.wait_ge(s15dve, jp + 1)
                                    sy.dma_start(woe[hp, :, qp * 512:(qp + 1) * 512],
                                                 woeb[:, jp % 2, :]).then_inc(s15st, 16)
                                nj += 1
                        for j in (NJ - 2, NJ - 1):
                            h, qt = wo_piece(j)
                            sy.wait_ge(s15dve, j + 1)
                            sy.dma_start(woe[h, :, qt * 512:(qt + 1) * 512],
                                         woeb[:, j % 2, :]).then_inc(s15st, 16)
                        sy.wait_ge(s15st, 16 * NJ)

                    @blk.gpsimd
                    def _(g):
                        for fc in range(FC):
                            if fc >= 2:
                                g.wait_ge(s3pe, 2 * (fc - 1))
                            g.dma_start(wgbuf[:, fc % 2, :, :], wg[fc]).then_inc(s3wld, 16)
                            g.dma_start(wubuf[:, fc % 2, :, :], wu[fc]).then_inc(s3wld, 16)
                            for cc in (2 * fc, 2 * fc + 1):
                                if cc < FC:
                                    g.dma_start(wdbf[cc], wd[cc]).then_inc(s3wd, 16)
                        g.wait_ge(s3wd, 16 * FC)

                    @blk.tensor
                    def _(t):
                        for fc in range(FC):
                            t.wait_ge(s3wld, 32 * (fc + 1))
                            if fc >= 1:
                                t.wait_ge(s3act, fc)
                            for dc in range(DC):
                                for sub in range(4):
                                    ins = t.matmul(pg[:, sub * 512:(sub + 1) * 512],
                                                   wgbuf[:, fc % 2, dc, :],
                                                   xnT[:, dc, sub * 512:(sub + 1) * 512],
                                                   start=(dc == 0), stop=(dc == DC - 1))
                            ins.then_inc(s3pe, 1)
                            if fc >= 1:
                                t.wait_ge(s3dve, fc)
                            for dc in range(DC):
                                for sub in range(4):
                                    ins = t.matmul(pu[:, sub * 512:(sub + 1) * 512],
                                                   wubuf[:, fc % 2, dc, :],
                                                   xnT[:, dc, sub * 512:(sub + 1) * 512],
                                                   start=(dc == 0), stop=(dc == DC - 1))
                            ins.then_inc(s3pe, 1)

                    @blk.scalar
                    def _(s):
                        for fc in range(FC):
                            s.wait_ge(s3pe, 2 * fc + 1)
                            if fc >= 1:
                                s.wait_ge(s3dve, fc)
                            s.activation(silb[:, :], pg[:, :], AF.Silu).then_inc(s3act, 1)
                            s.wait_ge(s3dve, fc + 1)
                            s.dma_start(hT[fc], hbuf[:, fc % 2, :]).then_inc(s3st, 16)
                        s.wait_ge(s3st, 16 * FC)

                    @blk.vector
                    def _(v):
                        nj = 0
                        for fc in range(FC):
                            v.wait_ge(s3act, fc + 1)
                            v.wait_ge(s3pe, 2 * fc + 2)
                            if fc >= 2:
                                v.wait_ge(s3st, 16 * (fc - 1))
                            v.tensor_mul(hbuf[:, fc % 2, :], silb[:, :], pu[:, :]).then_inc(s3dve, 1)
                            while nj < NJ and nj * FC < (fc + 1) * NJ:
                                j = nj
                                q = j % 2
                                v.wait_ge(s15ld, 64 * (j + 1))
                                v.tensor_add(wo4[:, q, 0, :], wo4[:, q, 0, :], wo4[:, q, 1, :])
                                v.tensor_add(wo4[:, q, 2, :], wo4[:, q, 2, :], wo4[:, q, 3, :])
                                if j >= 2:
                                    v.wait_ge(s15st, 16 * (j - 1))
                                v.tensor_add(woeb[:, q, :], wo4[:, q, 0, :],
                                             wo4[:, q, 2, :]).then_inc(s15dve, 1)
                                nj += 1
                        assert nj == NJ

        with ExitStack() as st4:
            E4 = st4.enter_context
            retr_s = E4(nc.sbuf_tensor("retr_s4", [128, H * TPC], BF16))
            acc = E4(nc.sbuf_tensor("acc", [128, NT * SLAB], F32))
            woeff = E4(nc.sbuf_tensor("woeff", [128, H, SLAB], BF16))
            hbuf4 = E4(nc.sbuf_tensor("hbuf4", [128, NSLOT, TPC], BF16))
            wdbuf4 = E4(nc.sbuf_tensor("wdbuf4", [128, NSLOT, SLAB], BF16))
            ppall = E4(nc.psum_tensor("ppall", [128, 4 * SLAB], F32))
            pp = [ppall[:, k * SLAB:(k + 1) * SLAB] for k in range(4)]
            s4hld = E4(nc.semaphore("s4hld"))
            s4wld = E4(nc.semaphore("s4wld"))
            s4pe = E4(nc.semaphore("s4pe"))
            s4fl = E4(nc.semaphore("s4fl"))
            s4wo = E4(nc.semaphore("s4wo"))
            s4acc = E4(nc.semaphore("s4acc"))
            s4out = E4(nc.semaphore("s4out"))
            s4outb = E4(nc.semaphore("s4outb"))
            s4retr = E4(nc.semaphore("s4retr"))
            s4xc = E4(nc.semaphore("s4xc"))

            def chunk_group(c):
                return 0 if c < FBS[0] else (c - FBS[0]) // 7 + 1

            def slot_wait_val(c_glob):
                p = c_glob - NSLOT
                if p < 0:
                    return 0
                sp, cp = p // FC, p % FC
                return (sp * NGRP + chunk_group(cp) + 1) * 16

            CHK_AHEAD = sum(FBS[:2])

            with nc.Block() as blk:
                @blk.sync
                def _(sy):
                    def load_chunks(s, c_lo, c_hi):
                        for c in range(c_lo, c_hi):
                            c_glob = s * FC + c
                            w = slot_wait_val(c_glob)
                            if w > 0:
                                sy.wait_ge(s4pe, w)
                            sy.dma_start(hbuf4[:, c_glob % NSLOT, :],
                                         hT[c]).then_inc(s4hld, 16)
                            sy.dma_start(wdbuf4[:, c_glob % NSLOT, :],
                                         wdbf[c][:, s * SLAB:(s + 1) * SLAB]).then_inc(s4wld, 16)

                    for s in range(NSLAB):
                        load_chunks(s, 0, FC)
                    sy.wait_ge(s4out, 16 * NSLAB * NT)

                @blk.gpsimd
                def _(g):
                    for s in range(NSLAB):
                        for t in range(NT):
                            g.dma_start(out[t * 128:(t + 1) * 128, s * SLAB:(s + 1) * SLAB],
                                        x[t * 128:(t + 1) * 128, s * SLAB:(s + 1) * SLAB]
                                        ).then_inc(s4xc, 16)
                    for s in range(NSLAB):
                        for t in range(NT):
                            g.wait_ge(s4fl, (s * NGRP + NGRP - 1) * 8 + t // 2 + 1)
                            g.dma_start(out[t * 128:(t + 1) * 128, s * SLAB:(s + 1) * SLAB],
                                        acc[:, t * SLAB:(t + 1) * SLAB],
                                        accum_op=ALU.add).then_inc(s4out, 16)
                    g.wait_ge(s4out, 16 * NSLAB * NT)
                    g.wait_ge(s4xc, 16 * NSLAB * NT)

                @blk.scalar
                def _(sc):
                    woe_p = woe.rearrange("h p o -> p h o")
                    sc.dma_start(retr_s[:, :], retr).then_inc(s4retr, 16)
                    sc.dma_start(woeff[:, :, :], woe_p[:, :, 0:SLAB]).then_inc(s4wo, 16)
                    for s in range(NSLAB - 1):
                        sc.wait_ge(s4pe, (s * NGRP + 2) * 16)
                        sc.dma_start(woeff[:, :, :],
                                     woe_p[:, :, (s + 1) * SLAB:(s + 2) * SLAB]
                                     ).then_inc(s4wo, 16)
                    sc.wait_ge(s4wo, 16 * NSLAB)

                @blk.tensor
                def _(t_):
                    for s in range(NSLAB):
                        for g, fb in enumerate(FBS):
                            chunks = [sum(FBS[:g]) + k for k in range(fb)]
                            cum = s * FC + chunks[-1] + 1
                            cum_sl = min(cum + 2, NSLAB * FC)
                            t_.wait_ge(s4hld, 16 * cum_sl)
                            t_.wait_ge(s4wld, 16 * cum_sl)
                            if g == 1:
                                t_.wait_ge(s4wo, 16 * (s + 1))
                                t_.wait_ge(s4retr, 16)

                            for t in range(NT):
                                kg = (s * NGRP + g) * 16 + t
                                kgp = (s * NGRP + g) * 8 + t // 2
                                if kg >= 4:
                                    t_.wait_ge(s4fl, kgp - 1)
                                ps = pp[t % 4]
                                per_n = (8 if g == 1 else 0) + fb
                                seen = [0, 0]
                                if g == 1:
                                    for h in range(H):
                                        for n in range(2):
                                            seen[n] += 1
                                            ins = t_.matmul(
                                                ps[:, n * 512:(n + 1) * 512],
                                                retr_s[:, h * TPC + t * 128: h * TPC + (t + 1) * 128],
                                                woeff[:, h, n * 512:(n + 1) * 512],
                                                start=(seen[n] == 1), stop=(seen[n] == per_n))
                                for c in chunks:
                                    sl = (s * FC + c) % NSLOT
                                    for n in range(2):
                                        seen[n] += 1
                                        ins = t_.matmul(
                                            ps[:, n * 512:(n + 1) * 512],
                                            hbuf4[:, sl, t * 128:(t + 1) * 128],
                                            wdbuf4[:, sl, n * 512:(n + 1) * 512],
                                            start=(seen[n] == 1), stop=(seen[n] == per_n))
                                ins.then_inc(s4pe, 1)

                @blk.vector
                def _(v):
                    for s in range(NSLAB):
                        for g in range(NGRP):
                            for t in range(0, NT, 2):
                                v.wait_ge(s4pe, (s * NGRP + g) * 16 + t + 2)
                                src_ap = ppall[:, (t % 4) * SLAB:(t % 4 + 2) * SLAB]
                                dst = acc[:, t * SLAB:(t + 2) * SLAB]
                                if g == 0:
                                    if s > 0:
                                        v.wait_ge(s4out, 16 * ((s - 1) * NT + t + 2))
                                    v.tensor_copy(dst, src_ap).then_inc(s4fl, 1)
                                else:
                                    v.tensor_tensor(dst, dst, src_ap,
                                                    ALU.add).then_inc(s4fl, 1)
    return nc


def _prep_inputs(L0_attn_hidden_states, L0_attn_Wq, L0_attn_bq, L0_attn_Wo,
                 L0_attn_M, L0_attn_z, L0_ffn_W_gate, L0_ffn_W_up, L0_ffn_W_down):
    f32 = lambda a: np.asarray(a, dtype=np.float32)
    hs = f32(L0_attn_hidden_states).reshape(-1, D)
    Wq = f32(L0_attn_Wq)
    bqv = f32(L0_attn_bq)
    Wo = f32(L0_attn_Wo)
    M = np.ascontiguousarray(f32(L0_attn_M))
    z = f32(L0_attn_z)
    Wg = f32(L0_ffn_W_gate)
    Wu = f32(L0_ffn_W_up)
    Wd = f32(L0_ffn_W_down)

    A = Wq.reshape(32, 128, D)[::4]
    wq_t = np.ascontiguousarray(A.reshape(H, 128, DC, 128).transpose(0, 3, 2, 1))
    bq_t = np.ascontiguousarray(bqv.reshape(32, 128)[::4].T)
    z_t = np.ascontiguousarray(z.T)
    wo_t = np.ascontiguousarray(Wo.T).reshape(H, 4, 128, D)
    wg_t = np.ascontiguousarray(Wg.reshape(FC, 128, DC, 128).transpose(0, 3, 2, 1))
    wu_t = np.ascontiguousarray(Wu.reshape(FC, 128, DC, 128).transpose(0, 3, 2, 1))
    wd_t = np.ascontiguousarray(Wd.T).reshape(FC, 128, D)

    shared = {"wq": wq_t, "bq": bq_t, "zt": z_t, "mm": M, "wo": wo_t,
              "wg": wg_t, "wu": wu_t, "wd": wd_t}
    shards = [np.ascontiguousarray(hs[c * TPC:(c + 1) * TPC]) for c in range(NCORES)]
    return shared, shards


def kernel(L0_attn_hidden_states, L0_attn_Wq, L0_attn_bq, L0_attn_Wk, L0_attn_bk,
           L0_attn_Wv, L0_attn_bv, L0_attn_Wo, L0_attn_M, L0_attn_z,
           L0_ffn_W_gate, L0_ffn_W_up, L0_ffn_W_down):
    shared, shards = _prep_inputs(L0_attn_hidden_states, L0_attn_Wq, L0_attn_bq,
                                  L0_attn_Wo, L0_attn_M, L0_attn_z,
                                  L0_ffn_W_gate, L0_ffn_W_up, L0_ffn_W_down)
    nc = build_nc()
    in_maps = [dict(shared, x=shards[c]) for c in range(NCORES)]
    res = run_bass_kernel_spmd(nc, in_maps, core_ids=list(range(NCORES)))
    outp = np.concatenate([res.results[c]["out"] for c in range(NCORES)], axis=0)
    return outp.reshape(2, 8192, D)


# version 14 (final)
# speedup vs baseline: 1.0925x; 1.0526x over previous
import numpy as np
import concourse.bass as bass
import concourse.mybir as mybir
from concourse.bass_utils import run_bass_kernel_spmd
from contextlib import ExitStack

F32 = mybir.dt.float32
BF16 = mybir.dt.bfloat16
AF = mybir.ActivationFunctionType
ALU = mybir.AluOpType
AX = mybir.AxisListType

NCORES = 8
TPC = 2048
NT = TPC // 128
D = 4096
DC = D // 128
DFF = 11008
FC = DFF // 128
H = 8
SLAB = 1024
NSLAB = D // SLAB
FBS = [2] + [7] * 12
NGRP = len(FBS)
NSLOT = 15
OPT_P4B = True
OPT_P4E = True
EPS_LN = 1e-5
EPS_DEN = 1e-6


def build_nc(dbg=False):
    nc = bass.Bass("TRN2", target_bir_lowering=False)
    x = nc.dram_tensor("x", [TPC, D], F32, kind="ExternalInput").ap()
    wq = nc.dram_tensor("wq", [H, 128, DC, 128], F32, kind="ExternalInput").ap()
    bq = nc.dram_tensor("bq", [128, H], F32, kind="ExternalInput").ap()
    zt = nc.dram_tensor("zt", [128, H], F32, kind="ExternalInput").ap()
    mm = nc.dram_tensor("mm", [H, 128, 128], F32, kind="ExternalInput").ap()
    wo = nc.dram_tensor("wo", [H, 4, 128, D], F32, kind="ExternalInput").ap()
    wg = nc.dram_tensor("wg", [FC, 128, DC, 128], F32, kind="ExternalInput").ap()
    wu = nc.dram_tensor("wu", [FC, 128, DC, 128], F32, kind="ExternalInput").ap()
    wd = nc.dram_tensor("wd", [FC, 128, D], F32, kind="ExternalInput").ap()
    out = nc.dram_tensor("out", [TPC, D], F32, kind="ExternalOutput").ap()

    scratch_kind = "ExternalOutput" if dbg else "Internal"
    hT = nc.dram_tensor("hT_s", [FC, 128, TPC], BF16, kind=scratch_kind).ap()
    woe = nc.dram_tensor("woe_s", [H, 128, D], BF16, kind=scratch_kind).ap()
    retr = nc.dram_tensor("retr_s", [128, H * TPC], BF16, kind=scratch_kind).ap()
    wdbf = nc.dram_tensor("wdbf_s", [FC, 128, D], BF16, kind="Internal").ap()

    with ExitStack() as top:
        E = top.enter_context
        ident = E(nc.sbuf_tensor("ident", [128, 128], BF16))
        ones1 = E(nc.sbuf_tensor("ones1", [1, 128], BF16))
        bqT = E(nc.sbuf_tensor("bqT", [128, H], F32))
        zT = E(nc.sbuf_tensor("zT", [128, H], BF16))
        M_s = E(nc.sbuf_tensor("M_s", [128, H, 128], BF16))
        epsb = E(nc.sbuf_tensor("epsb", [1, 1], F32))
        s0 = E(nc.semaphore("s0"))

        with nc.Block() as blk:
            @blk.gpsimd
            def _(g):
                g.dma_start(zT[:, :], zt).then_inc(s0, 16)
                g.dma_start(M_s[:, :, :], mm.rearrange("h d e -> d h e")).then_inc(s0, 16)
                g.memset(ident[:, :], 0.0)
                g.affine_select(out=ident[:, :], in_=ident[:, :],
                                compare_op=ALU.not_equal, fill=1.0, base=0,
                                pattern=[[-1, 128]], channel_multiplier=1)
                g.memset(ones1[:, :], 1.0)
                g.memset(epsb[:, :], EPS_DEN)
                g.wait_ge(s0, 48)

            @blk.sync
            def _(sy):
                sy.dma_start(bqT[:, :], bq).then_inc(s0, 16)

        with ExitStack() as mid:
            xnT = mid.enter_context(nc.sbuf_tensor("xnT", [128, DC, TPC], BF16))

            with ExitStack() as st1:
                E1 = st1.enter_context
                xbuf = E1(nc.sbuf_tensor("xbuf", [128, 3, D], F32))
                sqs = E1(nc.sbuf_tensor("sqs", [128, D // 2], F32))
                xnb = E1(nc.sbuf_tensor("xnb", [128, 2, D], BF16))
                ssum = E1(nc.sbuf_tensor("ssum", [128, NT], F32))
                ssqA = E1(nc.sbuf_tensor("ssqA", [128, NT], F32))
                ssqB = E1(nc.sbuf_tensor("ssqB", [128, NT], F32))
                sqv = E1(nc.sbuf_tensor("sqv", [128, NT], F32))
                stt = E1(nc.sbuf_tensor("stt", [128, NT, 6], F32))
                pt = [E1(nc.psum_tensor(f"pt{k}", [128, 1024], BF16)) for k in range(2)]
                s1xs = E1(nc.semaphore("s1xs"))
                s1xa = E1(nc.semaphore("s1xa"))
                s1rd = E1(nc.semaphore("s1rd"))
                s1rdo = E1(nc.semaphore("s1rdo"))
                s1sq = E1(nc.semaphore("s1sq"))
                s1var = E1(nc.semaphore("s1var"))
                s1sqrt = E1(nc.semaphore("s1sqrt"))
                s1st2 = E1(nc.semaphore("s1st2"))
                s1xn = E1(nc.semaphore("s1xn"))
                s1pet = E1(nc.semaphore("s1pet"))
                s1ev = E1(nc.semaphore("s1ev"))

                if dbg:
                    st_dbg = nc.dram_tensor("st_dbg", [128, NT * 9], F32,
                                            kind="ExternalOutput").ap()

                def xload_wait(eng, i, sweep):
                    prev = i - 3
                    if sweep == 0:
                        if prev < 0:
                            return
                        eng.wait_ge(s1rd, prev + 1)
                        eng.wait_ge(s1sq, prev + 1)
                    else:
                        if prev < 0:
                            p = max(j for j in range(NT - 3, NT) if j % 3 == i % 3)
                            eng.wait_ge(s1rd, p + 1)
                            eng.wait_ge(s1sq, p + 1)
                        else:
                            eng.wait_ge(s1xn, prev + 1)

                with nc.Block() as blk:
                    @blk.sync
                    def _(sy):
                        for sweep in range(2):
                            for i in range(0, NT, 2):
                                xload_wait(sy, i, sweep)
                                sy.dma_start(xbuf[:, i % 3, :],
                                             x[i * 128:(i + 1) * 128, :]).then_inc(s1xs, 16)
                        if dbg:
                            sy.wait_ge(s1xn, NT)
                            sy.dma_start(st_dbg[:, 0:NT], ssum[:, :]).then_inc(s1xs, 16)
                            sy.dma_start(st_dbg[:, 2 * NT:3 * NT], sqv[:, :]).then_inc(s1xs, 16)
                            sy.dma_start(st_dbg[:, 3 * NT:9 * NT],
                                         stt[:, :, :].rearrange("p a b -> p (a b)")
                                         ).then_inc(s1xs, 16)
                            sy.wait_ge(s1xs, 16 * (2 * (NT // 2) + 3))

                    @blk.vector
                    def _(v):
                        for i in range(NT):
                            if i % 2 == 0:
                                v.wait_ge(s1xs, 16 * (i // 2 + 1))
                            else:
                                v.wait_ge(s1xa, 16 * (i // 2 + 1))
                            v.reduce_sum(ssum[:, i:i + 1], xbuf[:, i % 3, :],
                                         axis=AX.X).then_inc(s1rd, 1)
                        v.wait_ge(s1sq, NT)
                        sm = lambda j: ssum[:, j:j + 1]
                        for j in range(NT):
                            v.tensor_scalar(stt[:, j, 0:1], sm(j), 1.0 / D, None, ALU.mult)
                        for j in range(NT):
                            v.tensor_mul(stt[:, j, 1:2], sm(j), sm(j))
                        for j in range(NT):
                            v.tensor_add(stt[:, j, 4:5], ssqA[:, j:j + 1], ssqB[:, j:j + 1])
                        for j in range(NT):
                            v.tensor_scalar(stt[:, j, 2:3], stt[:, j, 4:5],
                                            1.0 / D, EPS_LN, ALU.mult, ALU.add)
                        for j in range(NT):
                            v.scalar_tensor_tensor(stt[:, j, 3:4], stt[:, j, 1:2],
                                                   -1.0 / (float(D) * D), stt[:, j, 2:3],
                                                   ALU.mult, ALU.add).then_inc(s1var, 1)
                        for j in range(NT):
                            v.wait_ge(s1sqrt, j + 1)
                            ins = v.reciprocal(stt[:, j, 5:6], sqv[:, j:j + 1])
                        ins.then_inc(s1st2, 1)
                        for i in range(NT):
                            p = i % 2
                            if p == 0:
                                v.wait_ge(s1xs, 16 * (NT // 2 + i // 2 + 1))
                            else:
                                v.wait_ge(s1xa, 16 * (NT // 2 + i // 2 + 1))
                            if i >= 2:
                                v.wait_ge(s1pet, (i - 1) * 4)
                            v.tensor_scalar(xnb[:, p, :], xbuf[:, i % 3, :], stt[:, i, 0:1],
                                            stt[:, i, 5:6], ALU.subtract,
                                            ALU.mult).then_inc(s1xn, 1)

                    @blk.scalar
                    def _(s):
                        def odd_load(j, sweep):
                            xload_wait(s, j, sweep)
                            s.dma_start(xbuf[:, j % 3, :],
                                        x[j * 128:(j + 1) * 128, :]).then_inc(s1xa, 16)

                        odd_load(1, 0)
                        for i in range(NT):
                            if i % 2 == 0:
                                s.wait_ge(s1xs, 16 * (i // 2 + 1))
                            else:
                                s.wait_ge(s1xa, 16 * (i // 2 + 1))
                            s.activation(sqs[:, :], xbuf[:, i % 3, 0:D // 2], AF.Square,
                                         accum_out=ssqA[:, i:i + 1])
                            s.activation(sqs[:, :], xbuf[:, i % 3, D // 2:D], AF.Square,
                                         accum_out=ssqB[:, i:i + 1]).then_inc(s1sq, 1)
                            if i + 3 < NT and (i + 3) % 2 == 1:
                                odd_load(i + 3, 0)
                        for j in range(NT):
                            s.wait_ge(s1var, j + 1)
                            s.activation(sqv[:, j:j + 1], stt[:, j, 3:4],
                                         AF.Sqrt).then_inc(s1sqrt, 1)
                        odd_load(1, 1)
                        for i in range(NT):
                            if i + 3 < NT and (i + 3) % 2 == 1:
                                odd_load(i + 3, 1)
                            for c8 in range(4):
                                s.wait_ge(s1pet, i * 4 + c8 + 1)
                                s.copy(
                                    xnT[:, c8 * 8:(c8 + 1) * 8, i * 128:(i + 1) * 128],
                                    pt[c8 % 2][:, :].rearrange("p (a b) -> p a b", b=128)
                                ).then_inc(s1ev, 1)

                    @blk.tensor
                    def _(t):
                        for i in range(NT):
                            p = i % 2
                            t.wait_ge(s1xn, i + 1)
                            for c8 in range(4):
                                gidx = i * 4 + c8
                                if gidx >= 2:
                                    t.wait_ge(s1ev, gidx - 1)
                                for k in range(8):
                                    c = c8 * 8 + k
                                    ins = t.transpose(pt[c8 % 2][:, k * 128:(k + 1) * 128],
                                                      xnb[:, p, c * 128:(c + 1) * 128],
                                                      ident[:, :])
                                ins.then_inc(s1pet, 1)

            if dbg:
                xnT_d = nc.dram_tensor("xnT_dbg", [128, DC * TPC], BF16,
                                       kind="ExternalOutput").ap()
                sdbg = mid.enter_context(nc.semaphore("sdbg"))
                with nc.Block() as blk:
                    @blk.sync
                    def _(sy):
                        sy.dma_start(xnT_d, xnT[:, :, :].rearrange("p a b -> p (a b)")
                                     ).then_inc(sdbg, 16)
                        sy.wait_ge(sdbg, 16)

            with ExitStack() as st2:
                E2 = st2.enter_context
                retrT = E2(nc.sbuf_tensor("retrT", [128, H * TPC], BF16))
                wqbuf = E2(nc.sbuf_tensor("wqbuf", [128, 2, DC, 128], BF16))
                qselb = E2(nc.sbuf_tensor("qselb", [128, TPC], BF16))
                relub = E2(nc.sbuf_tensor("relub", [128, TPC], BF16))
                expb = E2(nc.sbuf_tensor("expb", [128, TPC], BF16))
                tmin = E2(nc.sbuf_tensor("tmin", [128, TPC], F32))
                scr8 = E2(nc.sbuf_tensor("scr8", [128, TPC], F32))
                pq = [E2(nc.psum_tensor(f"pq{k}", [128, TPC], F32)) for k in range(2)]
                s2wq = E2(nc.semaphore("s2wq"))
                s2pe = E2(nc.semaphore("s2pe"))
                s2act = E2(nc.semaphore("s2act"))
                s2dve = E2(nc.semaphore("s2dve"))
                rdenb = qselb[0:1, :]
                lnrow = relub[0:1, :]

                pe_q, pe_num, pe_den, pe_bden = {}, {}, {}, {}
                cnt = 0
                for h in range(H):
                    if h >= 1:
                        pe_num[h - 1] = cnt + 1
                        cnt += 1
                    if h >= 1:
                        pe_den[h - 1] = cnt + 1
                        cnt += 1
                    cnt += 1
                    pe_q[h] = cnt
                    if h >= 1:
                        pe_bden[h - 1] = cnt + 1
                        cnt += 1
                pe_num[H - 1] = cnt + 1
                pe_den[H - 1] = cnt + 2
                pe_bden[H - 1] = cnt + 3

                with nc.Block() as blk:
                    @blk.gpsimd
                    def _(g):
                        for h in range(H):
                            if h >= 2:
                                g.wait_ge(s2pe, pe_q[h - 2])
                            g.dma_start(wqbuf[:, h % 2, :, :], wq[h]).then_inc(s2wq, 16)
                        g.dma_start(zT[:, :], zt).then_inc(s2wq, 16)
                        g.wait_ge(s2wq, 16 * (H + 1))

                    @blk.tensor
                    def _(t):
                        def q_part(h, dc_lo, dc_hi):
                            P = pq[h % 2]
                            for dc in range(dc_lo, dc_hi):
                                for sub in range(4):
                                    ins = t.matmul(P[:, sub * 512:(sub + 1) * 512],
                                                   wqbuf[:, h % 2, dc, :],
                                                   xnT[:, dc, sub * 512:(sub + 1) * 512],
                                                   start=(dc == 0), stop=(dc == DC - 1))
                            return ins

                        def mm4(P, lhsT, rhs_full):
                            for sub in range(4):
                                ins = t.matmul(P[:, sub * 512:(sub + 1) * 512], lhsT,
                                               rhs_full[:, sub * 512:(sub + 1) * 512],
                                               start=True, stop=True)
                            return ins

                        def emit_num(j):
                            t.wait_ge(s2dve, 4 * j + 2)
                            t.wait_ge(s2act, 4 * j + 1)
                            mm4(pq[j % 2], M_s[:, j, :], qselb).then_inc(s2pe, 1)

                        def emit_den(j):
                            t.wait_ge(s2dve, 4 * j + 3)
                            P = pq[j % 2]
                            for sub in range(4):
                                ins = t.matmul(P[0:1, sub * 512:(sub + 1) * 512],
                                               zT[:, j:j + 1],
                                               qselb[:, sub * 512:(sub + 1) * 512],
                                               start=True, stop=True)
                            ins.then_inc(s2pe, 1)

                        def emit_bden(j):
                            t.wait_ge(s2act, 4 * j + 4)
                            mm4(pq[j % 2], ones1[:, :], rdenb).then_inc(s2pe, 1)

                        for h in range(H):
                            t.wait_ge(s2wq, 16 * (h + 2))
                            if h >= 2:
                                t.wait_ge(s2dve, 4 * (h - 2) + 4)
                            q_part(h, 0, 16)
                            if h >= 1:
                                emit_num(h - 1)
                            q_part(h, 16, 28)
                            if h >= 1:
                                emit_den(h - 1)
                            q_part(h, 28, DC).then_inc(s2pe, 1)
                            if h >= 1:
                                emit_bden(h - 1)
                        emit_num(H - 1)
                        emit_den(H - 1)
                        emit_bden(H - 1)

                    @blk.scalar
                    def _(s):
                        for h in range(H):
                            s.wait_ge(s2pe, pe_q[h])
                            if h >= 1:
                                s.wait_ge(s2dve, 4 * (h - 1) + 2)
                            s.activation(relub[:, :], pq[h % 2][:, :], AF.Relu,
                                         bias=bqT[:, h:h + 1]).then_inc(s2act, 1)
                            s.wait_ge(s2dve, 4 * h + 1)
                            s.activation(expb[:, :], tmin[:, :], AF.Exp).then_inc(s2act, 1)
                            s.wait_ge(s2pe, pe_den[h])
                            s.activation(lnrow, pq[h % 2][0:1, :], AF.Ln,
                                         bias=epsb[0:1, 0:1]).then_inc(s2act, 1)
                            s.activation(rdenb, lnrow, AF.Exp, scale=-1.0).then_inc(s2act, 1)
                        s.wait_ge(s2dve, 4 * H)
                        s.dma_start(retr, retrT[:, :]).then_inc(s2act, 16)
                        s.wait_ge(s2act, 4 * H + 16)

                    @blk.vector
                    def _(v):
                        for h in range(H):
                            P = pq[h % 2]
                            v.wait_ge(s2pe, pe_q[h])
                            v.wait_ge(s2act, 4 * h + 1)
                            v.tensor_scalar(tmin[:, :], P[:, :], bqT[:, h:h + 1], 0.0,
                                            ALU.add, ALU.min).then_inc(s2dve, 1)
                            v.wait_ge(s2act, 4 * h + 2)
                            if h >= 1:
                                v.wait_ge(s2pe, pe_bden[h - 1])
                            v.tensor_add(qselb[:, :], expb[:, :], relub[:, :]).then_inc(s2dve, 1)
                            v.wait_ge(s2pe, pe_num[h])
                            v.tensor_copy(scr8[:, :], P[:, :]).then_inc(s2dve, 1)
                            v.wait_ge(s2pe, pe_bden[h])
                            v.tensor_mul(retrT[:, h * TPC:(h + 1) * TPC],
                                         scr8[:, :], P[:, :]).then_inc(s2dve, 1)

            NJ = 8 * H
            with ExitStack() as st3:
                E3 = st3.enter_context
                wgbuf = E3(nc.sbuf_tensor("wgbuf", [128, 2, DC, 128], BF16))
                wubuf = E3(nc.sbuf_tensor("wubuf", [128, 2, DC, 128], BF16))
                silb = E3(nc.sbuf_tensor("silb", [128, TPC], F32))
                hbuf = E3(nc.sbuf_tensor("hbuf", [128, 2, TPC], BF16))
                wo4 = E3(nc.sbuf_tensor("wo4", [128, 2, 4, 512], F32))
                woeb = E3(nc.sbuf_tensor("woeb", [128, 2, 512], BF16))
                pg = E3(nc.psum_tensor("pg", [128, TPC], F32))
                pu = E3(nc.psum_tensor("pu", [128, TPC], F32))
                s3wld = E3(nc.semaphore("s3wld"))
                s3pe = E3(nc.semaphore("s3pe"))
                s3act = E3(nc.semaphore("s3act"))
                s3dve = E3(nc.semaphore("s3dve"))
                s3st = E3(nc.semaphore("s3st"))
                s3wd = E3(nc.semaphore("s3wd"))
                s15ld = E3(nc.semaphore("s15ld"))
                s15dve = E3(nc.semaphore("s15dve"))
                s15st = E3(nc.semaphore("s15st"))

                def wo_piece(j):
                    return j // 8, j % 8

                with nc.Block() as blk:
                    @blk.sync
                    def _(sy):
                        nj = 0
                        for fc in range(FC + 30):
                            while nj < NJ and nj * FC < (fc + 1) * NJ:
                                j = nj
                                h, qt = wo_piece(j)
                                if j >= 2:
                                    sy.wait_ge(s15dve, j - 1)
                                for g in range(4):
                                    sy.dma_start(wo4[:, j % 2, g, :],
                                                 wo[h, g, :, qt * 512:(qt + 1) * 512]
                                                 ).then_inc(s15ld, 16)
                                if j >= 2:
                                    jp = j - 2
                                    hp, qp = wo_piece(jp)
                                    sy.wait_ge(s15dve, jp + 1)
                                    sy.dma_start(woe[hp, :, qp * 512:(qp + 1) * 512],
                                                 woeb[:, jp % 2, :]).then_inc(s15st, 16)
                                nj += 1
                        for j in (NJ - 2, NJ - 1):
                            h, qt = wo_piece(j)
                            sy.wait_ge(s15dve, j + 1)
                            sy.dma_start(woe[h, :, qt * 512:(qt + 1) * 512],
                                         woeb[:, j % 2, :]).then_inc(s15st, 16)
                        sy.wait_ge(s15st, 16 * NJ)

                    @blk.gpsimd
                    def _(g):
                        for fc in range(FC):
                            if fc >= 2:
                                g.wait_ge(s3pe, 2 * (fc - 1))
                            g.dma_start(wgbuf[:, fc % 2, :, :], wg[fc]).then_inc(s3wld, 16)
                            g.dma_start(wubuf[:, fc % 2, :, :], wu[fc]).then_inc(s3wld, 16)
                            for cc in (2 * fc, 2 * fc + 1):
                                if cc < FC:
                                    g.dma_start(wdbf[cc], wd[cc]).then_inc(s3wd, 16)
                        g.wait_ge(s3wd, 16 * FC)

                    @blk.tensor
                    def _(t):
                        for fc in range(FC):
                            t.wait_ge(s3wld, 32 * min(fc + 2, FC) if fc < 4 else 32 * (fc + 1))
                            if fc >= 1:
                                t.wait_ge(s3act, fc)
                            for dc in range(DC):
                                for sub in range(4):
                                    ins = t.matmul(pg[:, sub * 512:(sub + 1) * 512],
                                                   wgbuf[:, fc % 2, dc, :],
                                                   xnT[:, dc, sub * 512:(sub + 1) * 512],
                                                   start=(dc == 0), stop=(dc == DC - 1))
                            ins.then_inc(s3pe, 1)
                            if fc >= 1:
                                t.wait_ge(s3dve, fc)
                            for dc in range(DC):
                                for sub in range(4):
                                    ins = t.matmul(pu[:, sub * 512:(sub + 1) * 512],
                                                   wubuf[:, fc % 2, dc, :],
                                                   xnT[:, dc, sub * 512:(sub + 1) * 512],
                                                   start=(dc == 0), stop=(dc == DC - 1))
                            ins.then_inc(s3pe, 1)

                    @blk.scalar
                    def _(s):
                        for fc in range(FC):
                            s.wait_ge(s3pe, 2 * fc + 1)
                            if fc >= 1:
                                s.wait_ge(s3dve, fc)
                            s.activation(silb[:, :], pg[:, :], AF.Silu).then_inc(s3act, 1)
                            s.wait_ge(s3dve, fc + 1)
                            s.dma_start(hT[fc], hbuf[:, fc % 2, :]).then_inc(s3st, 16)
                        s.wait_ge(s3st, 16 * FC)

                    @blk.vector
                    def _(v):
                        nj = 0
                        for fc in range(FC):
                            v.wait_ge(s3act, fc + 1)
                            v.wait_ge(s3pe, 2 * fc + 2)
                            if fc >= 2:
                                v.wait_ge(s3st, 16 * (fc - 1))
                            v.tensor_mul(hbuf[:, fc % 2, :], silb[:, :], pu[:, :]).then_inc(s3dve, 1)
                            while nj < NJ and nj * FC < (fc + 1) * NJ:
                                j = nj
                                q = j % 2
                                v.wait_ge(s15ld, 64 * (j + 1))
                                v.tensor_add(wo4[:, q, 0, :], wo4[:, q, 0, :], wo4[:, q, 1, :])
                                v.tensor_add(wo4[:, q, 2, :], wo4[:, q, 2, :], wo4[:, q, 3, :])
                                if j >= 2:
                                    v.wait_ge(s15st, 16 * (j - 1))
                                v.tensor_add(woeb[:, q, :], wo4[:, q, 0, :],
                                             wo4[:, q, 2, :]).then_inc(s15dve, 1)
                                nj += 1
                        assert nj == NJ

        with ExitStack() as st4:
            E4 = st4.enter_context
            retr_s = E4(nc.sbuf_tensor("retr_s4", [128, H * TPC], BF16))
            acc = E4(nc.sbuf_tensor("acc", [128, NT * SLAB], F32))
            woeff = E4(nc.sbuf_tensor("woeff", [128, H, SLAB], BF16))
            hbuf4 = E4(nc.sbuf_tensor("hbuf4", [128, NSLOT, TPC], BF16))
            wdbuf4 = E4(nc.sbuf_tensor("wdbuf4", [128, NSLOT, SLAB], BF16))
            ppall = E4(nc.psum_tensor("ppall", [128, 4 * SLAB], F32))
            pp = [ppall[:, k * SLAB:(k + 1) * SLAB] for k in range(4)]
            s4hld = E4(nc.semaphore("s4hld"))
            s4wld = E4(nc.semaphore("s4wld"))
            s4pe = E4(nc.semaphore("s4pe"))
            s4fl = E4(nc.semaphore("s4fl"))
            s4wo = E4(nc.semaphore("s4wo"))
            s4acc = E4(nc.semaphore("s4acc"))
            s4out = E4(nc.semaphore("s4out"))
            s4outb = E4(nc.semaphore("s4outb"))
            s4retr = E4(nc.semaphore("s4retr"))
            s4xc = E4(nc.semaphore("s4xc"))

            def chunk_group(c):
                return 0 if c < FBS[0] else (c - FBS[0]) // 7 + 1

            def slot_wait_val(c_glob):
                p = c_glob - NSLOT
                if p < 0:
                    return 0
                sp, cp = p // FC, p % FC
                return (sp * NGRP + chunk_group(cp) + 1) * 16

            CHK_AHEAD = sum(FBS[:2])

            with nc.Block() as blk:
                @blk.sync
                def _(sy):
                    def load_chunks(s, c_lo, c_hi):
                        for c in range(c_lo, c_hi):
                            c_glob = s * FC + c
                            w = slot_wait_val(c_glob)
                            if w > 0:
                                sy.wait_ge(s4pe, w)
                            sy.dma_start(hbuf4[:, c_glob % NSLOT, :],
                                         hT[c]).then_inc(s4hld, 16)
                            sy.dma_start(wdbuf4[:, c_glob % NSLOT, :],
                                         wdbf[c][:, s * SLAB:(s + 1) * SLAB]).then_inc(s4wld, 16)

                    for s in range(NSLAB):
                        load_chunks(s, 0, FC)
                    sy.wait_ge(s4out, 16 * NSLAB * NT)

                @blk.gpsimd
                def _(g):
                    for s in range(NSLAB):
                        for t in range(NT):
                            g.dma_start(out[t * 128:(t + 1) * 128, s * SLAB:(s + 1) * SLAB],
                                        x[t * 128:(t + 1) * 128, s * SLAB:(s + 1) * SLAB]
                                        ).then_inc(s4xc, 16)
                    for s in range(NSLAB):
                        for t in range(NT):
                            g.wait_ge(s4fl, (s * NGRP + NGRP - 1) * 8 + t // 2 + 1)
                            g.dma_start(out[t * 128:(t + 1) * 128, s * SLAB:(s + 1) * SLAB],
                                        acc[:, t * SLAB:(t + 1) * SLAB],
                                        accum_op=ALU.add).then_inc(s4out, 16)
                    g.wait_ge(s4out, 16 * NSLAB * NT)
                    g.wait_ge(s4xc, 16 * NSLAB * NT)

                @blk.scalar
                def _(sc):
                    woe_p = woe.rearrange("h p o -> p h o")
                    sc.dma_start(retr_s[:, :], retr).then_inc(s4retr, 16)
                    sc.dma_start(woeff[:, :, :], woe_p[:, :, 0:SLAB]).then_inc(s4wo, 16)
                    for s in range(NSLAB - 1):
                        sc.wait_ge(s4pe, (s * NGRP + 2) * 16)
                        sc.dma_start(woeff[:, :, :],
                                     woe_p[:, :, (s + 1) * SLAB:(s + 2) * SLAB]
                                     ).then_inc(s4wo, 16)
                    sc.wait_ge(s4wo, 16 * NSLAB)

                @blk.tensor
                def _(t_):
                    for s in range(NSLAB):
                        for g, fb in enumerate(FBS):
                            chunks = [sum(FBS[:g]) + k for k in range(fb)]
                            cum = s * FC + chunks[-1] + 1
                            cum_sl = min(cum + 2, NSLAB * FC)
                            t_.wait_ge(s4hld, 16 * cum_sl)
                            t_.wait_ge(s4wld, 16 * cum_sl)
                            if g == 1:
                                t_.wait_ge(s4wo, 16 * (s + 1))
                                t_.wait_ge(s4retr, 16)

                            for t in range(NT):
                                kg = (s * NGRP + g) * 16 + t
                                kgp = (s * NGRP + g) * 8 + t // 2
                                if kg >= 4:
                                    t_.wait_ge(s4fl, kgp - 1)
                                ps = pp[t % 4]
                                per_n = (8 if g == 1 else 0) + fb
                                seen = [0, 0]
                                if g == 1:
                                    for h in range(H):
                                        for n in range(2):
                                            seen[n] += 1
                                            ins = t_.matmul(
                                                ps[:, n * 512:(n + 1) * 512],
                                                retr_s[:, h * TPC + t * 128: h * TPC + (t + 1) * 128],
                                                woeff[:, h, n * 512:(n + 1) * 512],
                                                start=(seen[n] == 1), stop=(seen[n] == per_n))
                                for c in chunks:
                                    sl = (s * FC + c) % NSLOT
                                    for n in range(2):
                                        seen[n] += 1
                                        ins = t_.matmul(
                                            ps[:, n * 512:(n + 1) * 512],
                                            hbuf4[:, sl, t * 128:(t + 1) * 128],
                                            wdbuf4[:, sl, n * 512:(n + 1) * 512],
                                            start=(seen[n] == 1), stop=(seen[n] == per_n))
                                ins.then_inc(s4pe, 1)

                @blk.vector
                def _(v):
                    for s in range(NSLAB):
                        for g in range(NGRP):
                            for t in range(0, NT, 2):
                                v.wait_ge(s4pe, (s * NGRP + g) * 16 + t + 2)
                                src_ap = ppall[:, (t % 4) * SLAB:(t % 4 + 2) * SLAB]
                                dst = acc[:, t * SLAB:(t + 2) * SLAB]
                                if g == 0:
                                    if s > 0:
                                        v.wait_ge(s4out, 16 * ((s - 1) * NT + t + 2))
                                    v.tensor_copy(dst, src_ap).then_inc(s4fl, 1)
                                else:
                                    v.tensor_tensor(dst, dst, src_ap,
                                                    ALU.add).then_inc(s4fl, 1)
    return nc


def _prep_inputs(L0_attn_hidden_states, L0_attn_Wq, L0_attn_bq, L0_attn_Wo,
                 L0_attn_M, L0_attn_z, L0_ffn_W_gate, L0_ffn_W_up, L0_ffn_W_down):
    f32 = lambda a: np.asarray(a, dtype=np.float32)
    hs = f32(L0_attn_hidden_states).reshape(-1, D)
    Wq = f32(L0_attn_Wq)
    bqv = f32(L0_attn_bq)
    Wo = f32(L0_attn_Wo)
    M = np.ascontiguousarray(f32(L0_attn_M))
    z = f32(L0_attn_z)
    Wg = f32(L0_ffn_W_gate)
    Wu = f32(L0_ffn_W_up)
    Wd = f32(L0_ffn_W_down)

    A = Wq.reshape(32, 128, D)[::4]
    wq_t = np.ascontiguousarray(A.reshape(H, 128, DC, 128).transpose(0, 3, 2, 1))
    bq_t = np.ascontiguousarray(bqv.reshape(32, 128)[::4].T)
    z_t = np.ascontiguousarray(z.T)
    wo_t = np.ascontiguousarray(Wo.T).reshape(H, 4, 128, D)
    wg_t = np.ascontiguousarray(Wg.reshape(FC, 128, DC, 128).transpose(0, 3, 2, 1))
    wu_t = np.ascontiguousarray(Wu.reshape(FC, 128, DC, 128).transpose(0, 3, 2, 1))
    wd_t = np.ascontiguousarray(Wd.T).reshape(FC, 128, D)

    shared = {"wq": wq_t, "bq": bq_t, "zt": z_t, "mm": M, "wo": wo_t,
              "wg": wg_t, "wu": wu_t, "wd": wd_t}
    shards = [np.ascontiguousarray(hs[c * TPC:(c + 1) * TPC]) for c in range(NCORES)]
    return shared, shards


def kernel(L0_attn_hidden_states, L0_attn_Wq, L0_attn_bq, L0_attn_Wk, L0_attn_bk,
           L0_attn_Wv, L0_attn_bv, L0_attn_Wo, L0_attn_M, L0_attn_z,
           L0_ffn_W_gate, L0_ffn_W_up, L0_ffn_W_down):
    shared, shards = _prep_inputs(L0_attn_hidden_states, L0_attn_Wq, L0_attn_bq,
                                  L0_attn_Wo, L0_attn_M, L0_attn_z,
                                  L0_ffn_W_gate, L0_ffn_W_up, L0_ffn_W_down)
    nc = build_nc()
    in_maps = [dict(shared, x=shards[c]) for c in range(NCORES)]
    res = run_bass_kernel_spmd(nc, in_maps, core_ids=list(range(NCORES)))
    outp = np.concatenate([res.results[c]["out"] for c in range(NCORES)], axis=0)
    return outp.reshape(2, 8192, D)
